# version 02 (final)
# baseline (speedup 1.0000x reference)
import numpy as np
import ml_dtypes
import concourse.bass as bass
import concourse.mybir as mybir
from concourse.bass_utils import run_bass_kernel_spmd
from concourse import library_config
from concourse.library_overlay import lower_extended_insts

F32 = mybir.dt.float32
BF16 = mybir.dt.bfloat16
U32 = mybir.dt.uint32
U8 = mybir.dt.uint8
I16 = mybir.dt.int16
AluOp = mybir.AluOpType
ACT = mybir.ActivationFunctionType
AX = mybir.AxisListType

S = 2048
D = 1024
E = 8
KSEL = 513
KTOT = 528
NIDX = 640
NT = 16
NBIS = 34
KB = 1024
SBUF_LIMIT = 212863


class Builder:
    def __init__(self, nc):
        self.nc = nc
        self.stream = {e: [] for e in ("sync", "scalar", "vector", "tensor",
                                       "gpsimd")}
        self.cnt = {}
        self.sems = {}

    def sem(self, name):
        self.sems[name] = self.nc.alloc_semaphore(f"sem_{name}")
        self.cnt[name] = 0

    DMA_SEMS = ("ds", "dwi", "dw1a", "dw1b", "dw2a", "dw2b", "dza", "dzb",
                "dgz", "dgg", "dsc", "do")
    SELF_SEM = {"vector": "vt", "scalar": "sc"}

    def op(self, eng, fn, wait=(), inc=None, inc_n=None):
        own = self.SELF_SEM.get(eng)
        if inc is None and own is not None and fn is not None:
            inc = own
        if inc_n is None:
            inc_n = 16 if inc in self.DMA_SEMS else 1
        wait = [(s, v) for (s, v) in wait if v]
        selfw = self.cnt[own] if (own is not None and fn is not None) else 0
        self.stream[eng].append((tuple(wait), fn, inc, inc_n, own, selfw))
        if inc is not None:
            self.cnt[inc] += inc_n
            return self.cnt[inc]
        return None

    def finalize(self):
        sems = self.sems
        streams = self.stream

        def mk(name):
            def run(engobj):
                for wait, fn, inc, inc_n, own, selfw in streams[name]:
                    for (sname, val) in wait:
                        engobj.wait_ge(sems[sname], val)
                    if fn is not None:
                        ins = fn(engobj)
                        if inc is not None:
                            ins.then_inc(sems[inc], inc_n)
                        if own is not None and selfw > 0:
                            ins._wait_ge(sems[own], engobj.lower_val(selfw))
            return run

        with self.nc.Block() as block:
            block.sync(mk("sync"))
            block.scalar(mk("scalar"))
            block.vector(mk("vector"))
            block.tensor(mk("tensor"))
            block.gpsimd(mk("gpsimd"))


def build_nc(stage=3, debug=False, nexperts=8, cut=999, skip_padz=False):
    nc = bass.Bass("TRN2", target_bir_lowering=False)

    dkind = "ExternalOutput" if debug else "Internal"
    x_in = nc.dram_tensor("x_in", [S, D], F32, kind="ExternalInput").ap()
    wq_h = nc.dram_tensor("wq_h", [128, 8 * KB], F32, kind="ExternalInput").ap()
    wk_h = nc.dram_tensor("wk_h", [128, 8 * KB], F32, kind="ExternalInput").ap()
    wv_h = nc.dram_tensor("wv_h", [128, 8 * KB], F32, kind="ExternalInput").ap()
    rw_h = nc.dram_tensor("rw_h", [128, 64], F32, kind="ExternalInput").ap()
    w1_h = nc.dram_tensor("w1_h", [E, 4, 128, 8 * KB], F32, kind="ExternalInput").ap()
    b1_h = nc.dram_tensor("b1_h", [128, 256], F32, kind="ExternalInput").ap()
    w2_h = nc.dram_tensor("w2_h", [E, 4, 128, 8 * KB], F32, kind="ExternalInput").ap()
    b2_h = nc.dram_tensor("b2_h", [1, E * KB], F32, kind="ExternalInput").ap()
    iota1_h = nc.dram_tensor("iota1_h", [128, 128], F32, kind="ExternalInput").ap()
    identf_h = nc.dram_tensor("identf_h", [128, 128], F32, kind="ExternalInput").ap()
    identb_h = nc.dram_tensor("identb_h", [128, 128], BF16, kind="ExternalInput").ap()
    repl8_h = nc.dram_tensor("repl8_h", [8, 128], F32, kind="ExternalInput").ap()
    onesr_h = nc.dram_tensor("onesr_h", [1, 128], BF16, kind="ExternalInput").ap()
    onesrf_h = nc.dram_tensor("onesrf_h", [1, 128], F32, kind="ExternalInput").ap()
    ones64_h = nc.dram_tensor("ones64_h", [128, 1], BF16, kind="ExternalInput").ap()
    extr_h = nc.dram_tensor("extr_h", [128, 4], U8, kind="ExternalInput").ap()

    out_hbm = nc.dram_tensor("y_out", [S, D], F32, kind="ExternalOutput").ap()
    z_hbm = nc.dram_tensor("z_hbm", [S + 1, D], BF16, kind=dkind).ap()
    sc_hbm = nc.dram_tensor("sc_hbm", [S + 1, 64], F32, kind=dkind).ap()

    SBUF_BASE = 16512

    def at(name, shape, dtype, off):
        return nc.alloc_sbuf_tensor_at(name, shape, dtype,
                                       offset=SBUF_BASE + off).ap()

    cur = [0]

    def nxt(sz, align=64):
        cur[0] = (cur[0] + align - 1) // align * align
        r = cur[0]
        cur[0] += sz
        return r

    xe = at("xe", [128, 8 * KB], F32, nxt(32768))
    xo = at("xo", [128, 8 * KB], F32, nxt(32768))
    zring = at("zring", [128, 2 * KB], BF16, nxt(4096))
    stm = at("stm", [128, 128], F32, nxt(512))
    Osb = at("Osb", [65, 512], BF16, nxt(1024))
    topk128 = at("topk128", [128, 128], F32, nxt(512))
    iota1 = at("iota1", [128, 128], F32, nxt(512))
    mi = at("mi", [128, 132], F32, nxt(544))
    identb = at("identb", [128, 128], BF16, nxt(256))
    identf = at("identf", [128, 128], F32, nxt(512))
    repl8 = at("repl8", [8, 128], F32, nxt(512))
    onesr = at("onesr", [1, 128], BF16, nxt(256))
    onesrf = at("onesrf", [1, 128], F32, nxt(512))
    ones64 = at("ones64", [128, 1], BF16, nxt(32))
    b2bf = at("b2bf", [1, E * KB], BF16, nxt(16384))
    b1f = at("b1f", [128, 256], F32, nxt(1024))
    rwbf = at("rwbf", [128, 64], BF16, nxt(128))
    cb = at("cb", [128, 16], F32, nxt(64))
    lnstat = at("lnstat", [128, 64], F32, nxt(256))
    hmax = at("hmax", [1, 64], F32, nxt(256))
    bis = at("bis", [8, 16], F32, nxt(64))
    bcnd = at("bcnd", [8, 1], U8, nxt(32))
    thr128 = at("thr128", [128, 1], F32, nxt(32))
    sg8 = at("sg8", [16, 8 * 33], F32, nxt(1088))
    idxs = [at(f"idx{e}", [128, 40], I16, nxt(96)) for e in range(E)]
    sidxs = [at(f"sidx{e}", [128, 40], I16, nxt(96)) for e in range(E)]
    gg = at("gg", [128, 2 * 320], F32, nxt(2560))
    g05 = at("g05", [128, 16], F32, nxt(64))
    extr = at("extr", [128, 4], U8, nxt(32))
    c2048 = at("c2048", [128, 4], F32, nxt(32))
    nfb = at("nfb", [1, 8], U32, nxt(32))
    rec2 = at("rec2", [128, 2], F32, nxt(32))

    AR = (cur[0] + 511) // 512 * 512
    w1r = at("w1r", [128, 2 * 8 * KB], BF16, AR)
    w2r = at("w2r", [128, 2 * 8 * KB], BF16, AR + 32768)
    hTs = at("hTs", [128, 16 * KTOT], BF16, AR + 65536)
    ysb = at("ysb", [128, 5 * KB], F32, AR + 82432)
    gath = at("gath", [128, 8 * NIDX], BF16, AR + 102912)
    sc8 = at("sc8", [8, S], F32, AR + 65536)
    junk = at("junk", [8, S], F32, AR + 102912)
    mie = at("mie", [16, 8 * 132], F32, AR + 82432)
    discard = at("discard", [128, KB], F32, AR + 102912 + 4096)
    zerot = at("zerot", [128, KB], BF16, AR + 82432 + 12288)
    ffn_end = AR + 113152
    zT_f = at("zT_f", [128, 8 * S], BF16, AR + 65536)
    zT_a = at("zT_a", [128, 8 * S], BF16, AR)
    wbuf = at("wbuf", [128, 2 * 4 * KB], BF16, AR + 32768)
    qT = at("qT", [128, 4 * S], BF16, AR + 49152)
    kT = at("kT", [128, 4 * S], BF16, AR + 65536)
    vaug = at("vaug", [128, 16 * 520], BF16, AR + 81920)
    qsq = at("qsq", [128, 2 * S], BF16, AR + 98560)
    Ebuf = at("Ebuf", [128, 2 * KB], BF16, AR + 106752)
    attn_end = AR + 106752 + 4096
    assert max(ffn_end, attn_end) <= SBUF_LIMIT, (ffn_end, attn_end)

    PP = nc.alloc_psum_tensor("PP", [128, 4096], F32).ap()
    PPb = PP.bitcast(BF16)

    B = Builder(nc)
    for s in ("dwi", "dw1a", "dw1b", "dw2a", "dw2b", "dza", "dzb", "ds",
              "dgz", "dgg", "dsc", "vt", "sc", "te", "gp", "do"):
        B.sem(s)

    xv = {0: xe.rearrange("p (s c) -> p s c", c=KB),
          1: xo.rearrange("p (s c) -> p s c", c=KB)}

    def xtile(k):
        return xv[k % 2][:, k // 2, :]

    def zTc(zT, c):
        return zT.rearrange("p (c t) -> p c t", t=S)[:, c, :]

    MM = dict(skip_group_check=True)

    regs = {}

    def init_loads():
        m = {}
        def mkregs(e):
            regs["r528"] = e.to_reg(KTOT)
            regs["r0"] = e.to_reg(0)
            return e.engine_nop()
        B.op("gpsimd", mkregs)
        xin_t = x_in.rearrange("(a two p) d -> two p a d", two=2, p=128)
        for par in (0, 1):
            m[f"x{par}"] = B.op("sync", lambda e, par=par: e.dma_start(
                xv[par], xin_t[par]), inc="ds")
        for dst, src in [(iota1, iota1_h), (identf, identf_h), (identb, identb_h),
                         (repl8, repl8_h), (onesr, onesr_h), (onesrf, onesrf_h),
                         (ones64, ones64_h), (extr, extr_h), (b1f, b1_h)]:
            B.op("sync", lambda e, d=dst, s=src: e.dma_start(d, s), inc="ds")
        for dst, src in [(b2bf, b2_h), (rwbf, rw_h)]:
            B.op("gpsimd", lambda e, d=dst, s=src: e.dma_start(d, s), inc="dwi")
        B.op("vector", lambda e: e.memset(c2048, 2048.0))
        B.op("vector", lambda e: e.memset(gg, 0.0))
        B.op("vector", lambda e: e.memset(gath, 0.0))
        B.op("vector", lambda e: e.memset(ysb, 0.0))
        B.op("vector", lambda e: e.memset(zerot, 0.0))
        B.op("vector", lambda e: e.memset(hmax, 0.0))
        vz = B.op("vector", lambda e: e.memset(stm, 0.0), inc="vt")
        for ee in range(E):
            B.op("vector", lambda e, t=idxs[ee]: e.memset(t, -1))
            B.op("vector", lambda e, t=sidxs[ee]: e.memset(t, -1))
        vm = B.op("vector", lambda e: e.memset(mi[:, 128:132], -1.0), inc="vt")
        B.op("sync", lambda e: e.dma_start(z_hbm[S:S + 1, :], zerot[0:1, :]),
             wait=[("vt", vz)], inc="ds")
        zf = zerot.bitcast(F32)
        if not skip_padz:
            for r in range(16):
                B.op("sync", lambda e, r=r: e.dma_start(
                    sc_hbm[r * 128:(r + 1) * 128, 8:64], zf[:, 0:56]), inc="ds")
        B.op("sync", lambda e: e.dma_start(sc_hbm[S:S + 1, 0:64], zerot.bitcast(F32)[0:1, 0:64]),
             inc="ds")
        m["consts"] = B.cnt["ds"]
        m["vinit"] = vm
        return m

    def layernorm(zT, write_z, add_x, wait_x_v, wait_x_s, wait_psum_te):
        sums = lnstat[:, 0:16]
        sq = lnstat[:, 16:32]
        rstd = lnstat[:, 32:48]
        negmr = lnstat[:, 48:64]
        last_s = None
        for k in range(NT):
            B.op("vector", lambda e, k=k: e.reduce_sum(
                sums[:, k:k + 1], xtile(k), axis=AX.X),
                wait=wait_x_v if k == 0 else ())
            last_s = B.op("scalar", lambda e, k=k: e.activation(
                discard, xtile(k), ACT.Square, accum_out=sq[:, k:k + 1]),
                wait=wait_x_s if k == 0 else (), inc="sc")
        B.op("vector", lambda e: e.tensor_scalar(
            sums, sums, 1.0 / D, None, op0=AluOp.mult), wait=[("sc", last_s)])
        B.op("vector", lambda e: e.tensor_scalar(
            sq, sq, 1.0 / D, None, op0=AluOp.mult))
        B.op("vector", lambda e: e.tensor_tensor(negmr, sums, sums, op=AluOp.mult))
        B.op("vector", lambda e: e.tensor_sub(sq, sq, negmr))
        B.op("vector", lambda e: e.tensor_scalar(sq, sq, 1e-5, None, op0=AluOp.add))
        vrec = B.op("vector", lambda e: e.reciprocal(sq, sq), inc="vt")
        ssq = B.op("scalar", lambda e: e.activation(rstd, sq, ACT.Sqrt),
                   wait=[("vt", vrec)], inc="sc")
        B.op("vector", lambda e: e.tensor_tensor(negmr, sums, rstd, op=AluOp.mult),
             wait=[("sc", ssq)])
        vfin = B.op("vector", lambda e: e.tensor_scalar(
            negmr, negmr, -1.0, None, op0=AluOp.mult), inc="vt")

        te_m = [0] * NT
        ds_m = [0] * NT
        sc_m = [0] * NT
        for k in range(NT):
            zsem = "dza" if k % 2 == 0 else "dzb"
            zt = zring[:, (k % 2) * KB:(k % 2 + 1) * KB]
            w = [("vt", vfin)] if k == 0 else []
            if k >= 2:
                if zT is not None:
                    w.append(("te", te_m[k - 2]))
                if write_z:
                    w.append((zsem, ds_m[k - 2]))
            zm = B.op("vector", lambda e, k=k, zt=zt: e.tensor_scalar(
                zt, xtile(k), rstd[:, k:k + 1], negmr[:, k:k + 1],
                op0=AluOp.mult, op1=AluOp.add), wait=w, inc="vt")
            if add_x:
                B.op("vector", lambda e, k=k, zt=zt: e.tensor_tensor(
                    xtile(k), xtile(k), zt, op=AluOp.add), inc="vt")
            if write_z:
                ds_m[k] = B.op("sync", lambda e, k=k, zt=zt: e.dma_start(
                    z_hbm[k * 128:(k + 1) * 128, :], zt),
                    wait=[("vt", zm)], inc=zsem)
            if zT is not None:
                for half in range(2):
                    pb = PPb[:, (k % 2) * 1024:(k % 2) * 1024 + 1024]
                    for cc in range(4):
                        c = half * 4 + cc
                        tw = []
                        if cc == 0:
                            tw = [("vt", zm)]
                            if half == 1:
                                tw.append(("sc", sc_m[k]))
                            else:
                                if k == 0 and wait_psum_te:
                                    tw += wait_psum_te
                                if k >= 2:
                                    tw.append(("sc", sc_m[k - 2]))
                        r = B.op("tensor",
                                 lambda e, zt=zt, c=c, cc=cc, pb=pb: e.transpose(
                                     pb[:, cc * 128:(cc + 1) * 128],
                                     zt[:, c * 128:(c + 1) * 128], identb),
                                 wait=tw, inc="te" if cc == 3 else None)
                        if cc == 3:
                            te_m[k] = r
                    sc_m[k] = B.op("scalar",
                        lambda e, k=k, half=half, pb=pb: e.copy(
                            zT.rearrange("p (c t) -> p c t", t=S)
                            [:, half * 4:half * 4 + 4, k * 128:(k + 1) * 128],
                            pb[:, 0:512].rearrange("p (c f) -> p c f", f=128)),
                        wait=[("te", te_m[k])], inc="sc")
        vdone = B.op("vector", lambda e: e.engine_nop(), inc="vt")
        zw_tot = (B.cnt["dza"], B.cnt["dzb"]) if write_z else (0, 0)
        return (te_m[NT - 1], sc_m[NT - 1], zw_tot, vdone)

    def router_and_select(zT, zt_te, zt_sc, wpsum, cut=99):
        sumc = lnstat[:, 0:16]
        tgm = None
        for c in range(8):
            for ns in range(4):
                w = [("te", zt_te), ("sc", zt_sc)] if (c == 0 and ns == 0) else ()
                tgm = B.op("tensor", lambda e, c=c, ns=ns: e.matmul(
                    PP[0:8, ns * 512:(ns + 1) * 512],
                    rwbf[:, c * 8:(c + 1) * 8],
                    zTc(zT, c)[:, ns * 512:(ns + 1) * 512],
                    start=(c == 0), stop=(c == 7), **MM),
                    wait=w, inc="te" if (c == 7 and ns == 3) else None)
        lv = B.op("vector", lambda e: e.tensor_copy(junk, PP[0:8, 0:2048]),
                  wait=[("te", tgm)], inc="vt")
        if cut <= 11:
            return None, None
        smark = [0] * 16
        for k in range(16):
            pb = PP[:, 2048 + (k % 2) * 512: 2048 + (k % 2) * 512 + 128]
            w = [("vt", lv)] + (wpsum or []) if k == 0 else []
            if k >= 2:
                w.append(("sc", smark[k - 2]))
            t1 = B.op("tensor", lambda e, k=k, pb=pb: e.transpose(
                pb[:, 0:8], junk[:, k * 128:(k + 1) * 128], identf[0:8, 0:8]),
                wait=w, inc="te")
            smark[k] = B.op("scalar", lambda e, k=k, pb=pb: e.activation(
                stm[:, k * 8:k * 8 + 8], pb[:, 0:8], ACT.Exp,
                accum_out=sumc[:, k:k + 1]), wait=[("te", t1)], inc="sc")
        if cut <= 12:
            return None, None
        B.op("vector", lambda e: e.reciprocal(sumc, sumc),
             wait=[("sc", smark[-1])])
        nmark = [0] * 16
        for k in range(16):
            nmark[k] = B.op("vector", lambda e, k=k: e.tensor_scalar(
                stm[:, k * 8:k * 8 + 8], stm[:, k * 8:k * 8 + 8],
                sumc[:, k:k + 1], None, op0=AluOp.mult), inc="vt")
        if cut <= 13:
            return None, None
        scw = B.op("sync", lambda e: e.dma_start(
            sc_hbm[0:S, :].rearrange("(s p) g -> p s g", p=128)[:, :, 0:8],
            stm.rearrange("p (s g) -> p s g", g=8)),
            wait=[("vt", nmark[-1])], inc="ds")
        if cut <= 14:
            return None, None
        vmark = [0] * 16
        for k in range(16):
            if cut == 151 and k >= 1:
                break
            if cut == 157 and k >= 2:
                break
            if cut == 158 and k >= 4:
                break
            if cut == 159 and k >= 8:
                break
            pb = PP[0:8, 3072 + (k % 2) * 512: 3072 + (k % 2) * 512 + 128]
            w = [("vt", nmark[k])]
            if k >= 2:
                w.append(("vt", vmark[k - 2]))
            t1 = B.op("tensor", lambda e, k=k, pb=pb: e.transpose(
                pb[0:8, 0:128], stm[:, k * 8:k * 8 + 8], identf),
                wait=w, inc="te")
            if cut == 152:
                vmark[k] = B.op("vector", lambda e: e.engine_nop(),
                                wait=[("te", t1)], inc="vt")
                continue
            if cut == 153:
                vmark[k] = B.op("vector", lambda e, k=k, pb=pb: e.tensor_copy(
                    junk[:, k * 128:(k + 1) * 128], pb[0:8, 0:128]),
                    wait=[("te", t1)], inc="vt")
                continue
            if cut == 154:
                vmark[k] = B.op("scalar", lambda e, k=k, pb=pb: e.copy(
                    sc8[:, k * 128:(k + 1) * 128], pb[0:8, 0:128]),
                    wait=[("te", t1)], inc="sc")
                continue
            vmark[k] = B.op("vector", lambda e, k=k, pb=pb: e.tensor_copy(
                sc8[:, k * 128:(k + 1) * 128], pb[0:8, 0:128]),
                wait=[("te", t1)], inc="vt")
        if cut in (151, 152, 153, 154, 157, 158, 159):
            return None, None
        if cut <= 15:
            return None, None
        tk = B.op("sync", lambda e: e.dma_start(
            topk128, sc8.rearrange("e (a f) -> e a f", f=128)),
            wait=[("vt", vmark[-1])], inc="ds")
        if cut <= 16:
            return None, None

        LO, HI, MID, CNT, CND, T1, T2 = range(7)

        def col(i):
            return bis[:, i:i + 1]

        B.op("vector", lambda e: e.memset(col(LO), 0.0), wait=[("vt", vmark[-1])])
        B.op("vector", lambda e: e.memset(col(HI), 1.0))
        B.op("vector", lambda e: e.memset(col(MID), 0.5))
        for _ in range(NBIS):
            B.op("vector", lambda e: e.tensor_scalar(
                junk, sc8, col(MID), 0.0, op0=AluOp.is_ge, op1=AluOp.add,
                accum_out=col(CNT)))
            B.op("vector", lambda e: e.tensor_scalar(
                col(CND), col(CNT), float(KSEL), None, op0=AluOp.is_ge))
            B.op("vector", lambda e: e.tensor_tensor(
                col(T1), col(CND), col(MID), op=AluOp.mult))
            B.op("vector", lambda e: e.tensor_tensor(
                col(T2), col(CND), col(HI), op=AluOp.mult))
            B.op("vector", lambda e: e.tensor_max(col(LO), col(LO), col(T1)))
            B.op("vector", lambda e: e.tensor_max(col(HI), col(MID), col(T2)))
            B.op("vector", lambda e: e.tensor_add(col(MID), col(LO), col(HI)))
            B.op("vector", lambda e: e.tensor_scalar(
                col(MID), col(MID), 0.5, None, op0=AluOp.mult))
        vbis = B.op("vector", lambda e: e.engine_nop(), inc="vt")
        if cut <= 17:
            return None, None
        tb = B.op("tensor", lambda e: e.matmul(
            PP[:, 2560:2561], repl8, col(LO), start=True, stop=True, **MM),
            wait=[("vt", vbis)], inc="te")
        B.op("vector", lambda e: e.tensor_copy(thr128, PP[:, 2560:2561]),
             wait=[("te", tb)])
        B.op("vector", lambda e: e.tensor_scalar(
            mi[:, 0:128], topk128, thr128[:, 0:1], None, op0=AluOp.is_ge),
            wait=[("ds", tk)])
        B.op("vector", lambda e: e.tensor_tensor(
            mi[:, 0:128], mi[:, 0:128], iota1, op=AluOp.mult))
        B.op("vector", lambda e: e.tensor_scalar(
            mi[:, 0:128], mi[:, 0:128], 1.0, None, op0=AluOp.subtract))
        vmi = B.op("vector", lambda e: e.copy_predicated(
            mi[:, 128:132], extr, c2048), inc="vt")
        if cut <= 18:
            return None, None
        return vmi, scw

    def compaction(vmi, cut=99):
        stg = None
        for ee in range(E):
            stg = B.op("sync", lambda e, ee=ee: e.dma_start(
                mie[:, ee * 132:(ee + 1) * 132], mi[16 * ee:16 * ee + 16, :]),
                wait=[("vt", vmi)] if ee == 0 else (), inc="ds")
        if cut <= 191:
            return None
        B.op("gpsimd", lambda e: e.load_library(library_config.sparse_gather),
             wait=[("vt", vmi)])
        for ee in range(E):
            B.op("gpsimd", lambda e, ee=ee: e.sparse_gather(
                sg8[:, ee * 33:(ee + 1) * 33], mie[:, ee * 132:(ee + 1) * 132],
                num_found=nfb[0:1, ee:ee + 1]),
                wait=[("ds", stg)] if ee == 0 else ())
        gdr = B.op("gpsimd", lambda e: e.drain(), inc="gp")
        B.op("gpsimd", lambda e: e.load_library(library_config.mlp))
        if cut <= 192:
            return None
        rep = None
        for ee in range(E):
            B.op("vector", lambda e, ee=ee: e.tensor_copy(
                idxs[ee][0:16, 0:33], sg8[:, ee * 33:(ee + 1) * 33]),
                wait=[("gp", gdr)] if ee == 0 else ())
            cm = B.op("vector", lambda e, ee=ee: e.scalar_tensor_tensor(
                sidxs[ee][0:16, 0:33], sg8[:, ee * 33:(ee + 1) * 33], 2048.0,
                sg8[:, ee * 33:(ee + 1) * 33],
                op0=AluOp.not_equal, op1=AluOp.mult), inc="vt")
            B.op("sync", lambda e, ee=ee: e.dma_start(
                idxs[ee][16:32, :], idxs[ee][0:16, :]), wait=[("vt", cm)],
                inc="ds")
            rep = B.op("sync", lambda e, ee=ee: e.dma_start(
                sidxs[ee][16:32, :], sidxs[ee][0:16, :]), inc="ds")
        if cut <= 193:
            return None
        return [rep] * E

    ring_guard = {}

    def issue_quarters(ee, q, extra_wait=(), skip_w2=False):
        res = []
        for which, host, ring in ((1, w1_h, w1r), (2, w2_h, w2r)):
            if skip_w2 and which == 2:
                res.append(("dw2b", 0))
                continue
            sem = f"dw{which}{'a' if q % 2 == 0 else 'b'}"
            w = list(extra_wait)
            g = ring_guard.get((which, q % 2))
            if g:
                w.append(("te", g))
            dst = ring[:, (q % 2) * 8 * KB:(q % 2 + 1) * 8 * KB]
            cnt = B.op("gpsimd",
                       lambda e, d=dst, s=host[ee, q]: e.dma_start(d, s),
                       wait=w, inc=sem)
            res.append((sem, cnt))
        return res

    est = {"prev_expert_te": None, "ypar": 0, "prev_sct": None,
           "prev_vgate": None}

    def expert(ee, rmark, scw, zw, wq, jobs):
        gzw = [("ds", rmark), ("dza", zw[0]), ("dzb", zw[1])]
        if est["prev_expert_te"]:
            gzw.append(("te", est["prev_expert_te"]))
        gz = B.op("gpsimd", lambda e: e.dma_gather(
            gath.rearrange("p (c n) -> p c n", c=8),
            z_hbm, idxs[ee], NIDX, regs["r528"], D, transpose=True),
            wait=gzw, inc="dgz")
        ggs = gg[:, (ee % 2) * 320:(ee % 2) * 320 + 320]
        gtw = [("ds", rmark)]
        if est["prev_vgate"]:
            gtw.append(("vt", est["prev_vgate"]))
        gt = B.op("gpsimd", lambda e, ggs=ggs: e.dma_gather(
            ggs.rearrange("p (c g) -> p c g", g=64),
            sc_hbm, idxs[ee], NIDX, regs["r528"], 64, transpose=False),
            wait=gtw, inc="dgg")
        vgate = B.op("vector", lambda e, ggs=ggs: e.tensor_scalar(
            g05[:, (ee % 2) * 8:(ee % 2) * 8 + 5],
            ggs.rearrange("p (c g) -> p c g", g=64)[:, 0:5, ee],
            0.5, None, op0=AluOp.mult), wait=[("dgg", gt)], inc="vt")
        est["prev_vgate"] = vgate

        gmk = {}
        ymarks = []
        for q in range(4):
            jw1, jw2 = wq[(ee, q)]
            for hl in range(8):
                h = q * 8 + hl
                hb = PP[:, (h % 2) * 1024:(h % 2) * 1024 + 1024]
                mm = None
                for c in range(8):
                    for half, (n0, n1) in enumerate(((0, 512), (512, KTOT))):
                        w = ()
                        if c == 0 and half == 0:
                            w = [jw1, ("dgz", gz)]
                            if h >= 2:
                                w.append(("sc", gmk[h - 2]))
                        last = (c == 7 and half == 1)
                        mm = B.op("tensor",
                            lambda e, q=q, hl=hl, c=c, n0=n0, n1=n1, hb=hb:
                            e.matmul(
                                hb[:, n0:n1],
                                w1r[:, (q % 2) * 8 * KB + c * KB + hl * 128:
                                    (q % 2) * 8 * KB + c * KB + hl * 128 + 128],
                                gath.rearrange("p (c n) -> p c n", c=8)
                                [:, c, n0:n1],
                                start=(c == 0), stop=(c == 7), **MM),
                            wait=w, inc="te" if last else None)
                gmk[h] = B.op("scalar", lambda e, h=h, hb=hb, ee=ee: e.activation(
                    hTs[:, (h % 16) * KTOT:(h % 16 + 1) * KTOT], hb[:, 0:KTOT],
                    ACT.Gelu,
                    bias=b1f[:, ee * 32 + h:ee * 32 + h + 1]),
                    wait=[("te", mm)], inc="sc")
            ring_guard[(1, q % 2)] = B.cnt["te"]
            for t in range(5):
                tw_ = 128 if t < 4 else 16
                yb = PP[:, 2048 + (est["ypar"] % 2) * 1024:
                        2048 + (est["ypar"] % 2) * 1024 + 1024]
                est["ypar"] += 1
                first_w = [jw2, ("sc", gmk[q * 8 + 7])]
                if len(ymarks) >= 2:
                    first_w.append(("vt", ymarks[-2]))
                first = True
                mm = None
                for hl in range(8):
                    h = q * 8 + hl
                    for half, (n0, n1) in enumerate(((0, 512), (512, 1024))):
                        stop = (q != 0 and hl == 7 and half == 1)
                        mm = B.op("tensor",
                            lambda e, h=h, t=t, tw=tw_, q=q, hl=hl, n0=n0,
                            n1=n1, yb=yb, stop=stop: e.matmul(
                                yb[0:tw, n0:n1],
                                hTs[:, (h % 16) * KTOT + t * 128:
                                    (h % 16) * KTOT + t * 128 + tw],
                                w2r[:, (q % 2) * 8 * KB + hl * KB + n0:
                                    (q % 2) * 8 * KB + hl * KB + n1],
                                start=(hl == 0), stop=stop, **MM),
                            wait=first_w if first else (),
                            inc="te" if stop else None)
                        first = False
                if q == 0:
                    for bi, (n0, n1) in enumerate(((0, 512), (512, 1024))):
                        mm = B.op("tensor",
                            lambda e, t=t, tw=tw_, n0=n0, n1=n1, yb=yb,
                            ee=ee, bi=bi: e.matmul(
                                yb[0:tw, n0:n1], onesr[0:1, 0:tw],
                                b2bf[0:1, ee * KB + n0:ee * KB + n1],
                                start=False, stop=True, **MM),
                            inc="te" if bi == 1 else None)
                evw = [("te", mm)]
                if q == 0:
                    evw.append(("vt", vgate))
                    if est["prev_sct"]:
                        evw.append(("dsc", est["prev_sct"]))
                    ym = B.op("vector", lambda e, t=t, yb=yb, ee=ee:
                        e.tensor_scalar(
                            ysb[:, t * KB:(t + 1) * KB], yb[:, 0:KB],
                            g05[:, (ee % 2) * 8 + t:(ee % 2) * 8 + t + 1],
                            None, op0=AluOp.mult), wait=evw, inc="vt")
                else:
                    ym = B.op("vector", lambda e, t=t, yb=yb, ee=ee:
                        e.scalar_tensor_tensor(
                            ysb[:, t * KB:(t + 1) * KB], yb[:, 0:KB],
                            g05[:, (ee % 2) * 8 + t:(ee % 2) * 8 + t + 1],
                            ysb[:, t * KB:(t + 1) * KB],
                            op0=AluOp.mult, op1=AluOp.add), wait=evw, inc="vt")
                ymarks.append(ym)
            ring_guard[(2, q % 2)] = B.cnt["te"]
            nj = ee * 4 + q + 2
            if nj < len(jobs):
                wq[jobs[nj]] = issue_quarters(*jobs[nj])
        est["prev_expert_te"] = B.cnt["te"]
        sw = [("vt", ymarks[-1])]
        if est["prev_sct"]:
            sw.append(("dsc", est["prev_sct"]))
        sct = B.op("gpsimd", lambda e: e.dma_scatter_add(
            xe.rearrange("p (s c) -> p s c", c=KB),
            ysb.rearrange("p (c n) -> p c n", c=5),
            sidxs[ee], NIDX, regs["r528"], D,
            sbuf_tokens_per_rank=128, parity_reg=regs["r0"],
            out_ap_other=xo.rearrange("p (s c) -> p s c", c=KB)),
            wait=sw, inc="dsc")
        est["prev_sct"] = sct
        return sct

    def ffn_call(wait_x_v, wait_x_s, wait_psum_te):
        stamp = B.cnt["te"]
        for key in ((1, 0), (1, 1), (2, 0), (2, 1)):
            ring_guard[key] = max(ring_guard.get(key, 0), stamp)
        est["prev_expert_te"] = stamp
        jobs = [(ee, q) for ee in range(E) for q in range(4)]
        wq = {}
        wq[jobs[0]] = issue_quarters(*jobs[0])
        if cut == 31:
            wq[jobs[1]] = (("dw1b", 0), ("dw2b", 0))
        elif cut == 32:
            wq[jobs[1]] = (issue_quarters(*jobs[1], skip_w2=True)[0], ("dw2b", 0))
        else:
            wq[jobs[1]] = issue_quarters(*jobs[1])
        zt_te, zt_sc, zw, _ = layernorm(zT_f, True, False, wait_x_v, wait_x_s,
                                        wait_psum_te)
        if cut <= 1:
            return None
        vmi, scw = router_and_select(zT_f, zt_te, zt_sc, wait_psum_te, cut)
        if cut <= 17 or vmi is None:
            return None
        if cut == 38:
            rmarks = [0] * E
        else:
            rmarks = compaction(vmi, cut)
        if cut <= 3:
            return None

        sct = None
        for ee in range(E):
            if ee >= nexperts:
                break
            sct = expert(ee, rmarks[ee], scw, zw, wq, jobs)
        return sct

    def attention(wait_x_v, wait_x_s, wait_psum_te):
        zt_te, zt_sc, _, _ = layernorm(zT_a, False, True, wait_x_v, wait_x_s,
                                       wait_psum_te)
        stamp = B.cnt["te"]
        wguard = {0: stamp, 1: stamp}
        qk_ev = {}
        v_ev = {}
        ot_ev = {}
        qkp = [0]
        otp = [0]
        for grp in range(2):
            cbase = grp * 4
            wdma = {}
            for kind, host, slot in (("q", wq_h, 0), ("k", wk_h, 1)):
                sem = "dw1a" if slot == 0 else "dw1b"
                w = []
                if wguard.get(slot):
                    w.append(("te", wguard[slot]))
                for jt in range(4):
                    srcj = host.rearrange("p (c j) -> p c j", j=KB)[
                        :, :, (cbase + jt) * 128:(cbase + jt + 1) * 128]
                    dstj = wbuf[:, slot * 4 * KB + jt * KB:
                                slot * 4 * KB + (jt + 1) * KB].rearrange(
                        "p (c jj) -> p c jj", jj=128)
                    cntw = B.op("gpsimd", lambda e, d=dstj, s=srcj:
                                e.dma_start(d, s),
                                wait=w if jt == 0 else (), inc=sem)
                wdma[kind] = (sem, cntw)
            vo = B.op("vector", lambda e: e.memset(
                vaug.rearrange("p (kb h o) -> p kb h o", h=8, o=65)
                [:, :, :, 64:65], 1.0), inc="vt")
            for kind, dstT, slot in (("q", qT, 0), ("k", kT, 1)):
                for jc in range(4):
                    for ns in range(4):
                        pb = PP[:, 1024 + (qkp[0] % 2) * 512:
                                1536 + (qkp[0] % 2) * 512]
                        mm = None
                        for c in range(8):
                            w = ()
                            if c == 0:
                                w = [wdma[kind]]
                                if kind == "q" and jc == 0 and ns == 0:
                                    w += [("te", zt_te), ("sc", zt_sc)]
                                if qkp[0] >= 2:
                                    w.append(("sc", qk_ev[qkp[0] - 2]))
                            mm = B.op("tensor",
                                lambda e, jc=jc, ns=ns, c=c, pb=pb, slot=slot:
                                e.matmul(
                                    pb,
                                    wbuf[:, slot * 4 * KB + jc * KB + c * 128:
                                         slot * 4 * KB + jc * KB + c * 128 + 128],
                                    zTc(zT_a, c)[:, ns * 512:(ns + 1) * 512],
                                    start=(c == 0), stop=(c == 7), **MM),
                                wait=w, inc="te" if c == 7 else None)
                        qk_ev[qkp[0]] = B.op("scalar",
                            lambda e, jc=jc, ns=ns, pb=pb, dstT=dstT: e.copy(
                                dstT.rearrange("p (c t) -> p c t", t=S)
                                [:, jc, ns * 512:(ns + 1) * 512], pb),
                            wait=[("te", mm)], inc="sc")
                        qkp[0] += 1
                if kind == "q":
                    wguard[0] = B.cnt["te"]
                else:
                    wguard[1] = B.cnt["te"]
            wv_src = wv_h.rearrange("p (c j) -> p c j", j=KB)[
                :, :, cbase * 128:cbase * 128 + 512]
            wv_dst = wbuf[:, 0:4 * KB].rearrange("p (c j) -> p c j", j=512)
            wvj = B.op("gpsimd", lambda e, d=wv_dst, s=wv_src: e.dma_start(d, s),
                       wait=[("te", wguard[0])], inc="dw1a")
            for tt in range(16):
                pb = PP[:, 2048 + (tt % 2) * 512:2560 + (tt % 2) * 512]
                mm = None
                for c in range(8):
                    w = ()
                    if c == 0:
                        w = [("dw1a", wvj)]
                        if tt >= 2:
                            w.append(("sc", v_ev[tt - 2]))
                    mm = B.op("tensor", lambda e, tt=tt, c=c, pb=pb: e.matmul(
                        pb, zTc(zT_a, c)[:, tt * 128:(tt + 1) * 128],
                        wbuf[:, c * 512:(c + 1) * 512],
                        start=(c == 0), stop=(c == 7), **MM),
                        wait=w, inc="te" if c == 7 else None)
                v_ev[tt] = B.op("scalar", lambda e, tt=tt, pb=pb: e.copy(
                    vaug.rearrange("p (kb h o) -> p kb h o", h=8, o=65)
                    [:, tt, :, 0:64],
                    pb.rearrange("p (h j) -> p h j", j=64)),
                    wait=[("te", mm), ("vt", vo)], inc="sc")
            wguard[0] = B.cnt["te"]
            v_done = B.cnt["sc"]
            mseq = []
            bred = {}
            seq = 0
            for ki, Tm in ((0, qT), (1, kT)):
                for jc in range(4):
                    w = [("sc", v_done)] if seq < 2 else [("te", mseq[seq - 2])]
                    sqm = B.op("scalar", lambda e, Tm=Tm, seq=seq, jc=jc:
                        e.activation(
                            qsq[:, (seq % 2) * S:(seq % 2) * S + S],
                            Tm.rearrange("p (c t) -> p c t", t=S)[:, jc, :],
                            ACT.Square), wait=w, inc="sc")
                    last_mm = None
                    for r in range(2):
                        base = r * 2048
                        mm = None
                        for ns in range(4):
                            w2_ = ()
                            if ns == 0:
                                w2_ = [("sc", sqm)]
                                if bred.get(r):
                                    w2_.append(("vt", bred[r]))
                            mm = B.op("tensor",
                                lambda e, seq=seq, r=r, ns=ns, base=base:
                                e.matmul(
                                    PP[0:1, base + ns * 512:
                                       base + (ns + 1) * 512],
                                    ones64[r * 64:r * 64 + 64, 0:1],
                                    qsq[:, (seq % 2) * S:(seq % 2) * S + S]
                                    [r * 64:r * 64 + 64,
                                     ns * 512:(ns + 1) * 512],
                                    start=True, stop=True, **MM),
                                wait=w2_, inc="te" if ns == 3 else None)
                        h = (cbase + jc) * 2 + r
                        bred[r] = B.op("vector", lambda e, h=h, ki=ki,
                                       base=base: e.reduce_max(
                            hmax[0:1, ki * 16 + h:ki * 16 + h + 1],
                            PP[0:1, base:base + 2048], axis=AX.X),
                            wait=[("te", mm)], inc="vt")
                        last_mm = mm
                    mseq.append(last_mm)
                    seq += 1
            o32 = 32 + cbase * 2
            o48 = 48 + cbase * 2
            B.op("vector", lambda e, o32=o32, cbase=cbase: e.tensor_tensor(
                hmax[0:1, o32:o32 + 8], hmax[0:1, cbase * 2:cbase * 2 + 8],
                hmax[0:1, 16 + cbase * 2:16 + cbase * 2 + 8], op=AluOp.mult))
            vs = B.op("vector", lambda e: e.engine_nop(), inc="vt")
            ssq2 = B.op("scalar", lambda e, o32=o32, o48=o48: e.activation(
                hmax[0:1, o48:o48 + 8], hmax[0:1, o32:o32 + 8], ACT.Sqrt),
                wait=[("vt", vs)], inc="sc")
            vneg = B.op("vector", lambda e, o48=o48: e.tensor_scalar(
                hmax[0:1, o48:o48 + 8], hmax[0:1, o48:o48 + 8], -0.125, None,
                op0=AluOp.mult), wait=[("sc", ssq2)], inc="vt")
            tcb = B.op("tensor", lambda e, o48=o48: e.matmul(
                PP[:, 3072:3080], onesrf, hmax[0:1, o48:o48 + 8],
                start=True, stop=True, **MM), wait=[("vt", vneg)], inc="te")
            vcb = B.op("vector", lambda e, cbase=cbase: e.tensor_copy(
                cb[:, cbase * 2:cbase * 2 + 8], PP[:, 3072:3080]),
                wait=[("te", tcb)], inc="vt")

            prev_exp = {"last": 0, "l2": 0}
            prev_av = {"14": 0, "15": 0}
            prev_om = {"last": 0, "toms": 0}
            for hl in range(8):
                h = cbase * 2 + hl
                jc = hl // 2
                ho = (hl % 2) * 64
                for qs in range(2):
                    emarks = {}
                    avmarks = {}
                    avm = None
                    for kb in range(16):
                        sb = PP[:, (kb % 2) * 1024:(kb % 2) * 1024 + 1024]
                        mm = None
                        for ns in range(2):
                            w = ()
                            if ns == 0:
                                w = [("vt", vcb), ("sc", v_done)] \
                                    if (hl == 0 and qs == 0 and kb == 0) else []
                                if kb >= 2:
                                    w.append(("sc", emarks[kb - 2]))
                                elif kb == 0:
                                    w.append(("sc", prev_exp["l2"]))
                                else:
                                    w.append(("sc", prev_exp["last"]))
                            mm = B.op("tensor",
                                lambda e, jc=jc, ho=ho, kb=kb, qs=qs, ns=ns,
                                sb=sb: e.matmul(
                                    sb[:, ns * 512:(ns + 1) * 512],
                                    kT.rearrange("p (c t) -> p c t", t=S)
                                    [ho:ho + 64, jc, kb * 128:(kb + 1) * 128],
                                    qT.rearrange("p (c t) -> p c t", t=S)
                                    [ho:ho + 64, jc,
                                     qs * 1024 + ns * 512:
                                     qs * 1024 + (ns + 1) * 512],
                                    start=True, stop=True, **MM),
                                wait=w, inc="te" if ns == 1 else None)
                        eb = Ebuf[:, (kb % 2) * KB:(kb % 2) * KB + KB]
                        ew = [("te", mm)]
                        if kb >= 2:
                            ew.append(("te", avmarks[kb - 2]))
                        elif kb == 0:
                            ew.append(("te", prev_av["14"]))
                        else:
                            ew.append(("te", prev_av["15"]))
                        emarks[kb] = B.op("scalar",
                            lambda e, h=h, sb=sb, eb=eb: e.activation(
                                eb, sb[:, 0:1024], ACT.Exp, scale=0.125,
                                bias=cb[:, h:h + 1]), wait=ew, inc="sc")
                        for qc in range(2):
                            w = [("sc", emarks[kb])] if qc == 0 else []
                            if kb == 0 and qc == 0:
                                w.append(("sc", prev_om["last"]))
                            avm = B.op("tensor",
                                lambda e, h=h, kb=kb, qc=qc, eb=eb: e.matmul(
                                    PP[0:65, 2048 + qc * 512:2560 + qc * 512],
                                    vaug[:, kb * 520 + (h % 8) * 65:
                                         kb * 520 + (h % 8) * 65 + 65],
                                    eb[:, qc * 512:(qc + 1) * 512],
                                    start=(kb == 0), stop=(kb == 15), **MM),
                                wait=w, inc="te")
                        avmarks[kb] = avm
                    prev_exp["l2"] = emarks[14]
                    prev_exp["last"] = emarks[15]
                    prev_av["14"] = avmarks[14]
                    prev_av["15"] = avmarks[15]
                    for qc in range(2):
                        omw = [("te", avm)]
                        if qc == 1:
                            omw.append(("te", prev_om["toms"]))
                        om = B.op("scalar", lambda e, qc=qc: e.copy(
                            Osb, PP[0:65, 2048 + qc * 512:2560 + qc * 512]),
                            wait=omw, inc="sc")
                        prev_om["last"] = om
                        for tsl in range(4):
                            k = qs * 8 + qc * 4 + tsl
                            otb = PPb[:, 6144 + (otp[0] % 2) * 1024:
                                      6144 + (otp[0] % 2) * 1024 + 65]
                            w = [("sc", om)]
                            if otp[0] >= 2:
                                w.append(("vt", ot_ev[otp[0] - 2]))
                            tom = B.op("tensor",
                                lambda e, tsl=tsl, otb=otb: e.transpose(
                                    otb, Osb[:, tsl * 128:(tsl + 1) * 128],
                                    identb[0:65, 0:65]), wait=w, inc="te")
                            prev_om["toms"] = tom
                            rc = rec2[:, otp[0] % 2:otp[0] % 2 + 1]
                            B.op("vector", lambda e, otb=otb, rc=rc:
                                 e.reciprocal(rc, otb[:, 64:65]),
                                 wait=[("te", tom)])
                            ot_ev[otp[0]] = B.op("vector",
                                lambda e, k=k, h=h, otb=otb, rc=rc:
                                e.scalar_tensor_tensor(
                                    xtile(k)[:, h * 64:(h + 1) * 64],
                                    otb[:, 0:64], rc,
                                    xtile(k)[:, h * 64:(h + 1) * 64],
                                    op0=AluOp.mult, op1=AluOp.add), inc="vt")
                            otp[0] += 1
            wguard[0] = B.cnt["te"]
            wguard[1] = B.cnt["te"]

    im = init_loads()
    wx = [("ds", im["consts"]), ("vt", im["vinit"]), ("dwi", B.cnt["dwi"])]
    ffn_call(wx, wx, None)
    if stage >= 2:
        wx2 = [("dsc", B.cnt["dsc"]), ("vt", B.cnt["vt"])]
        attention(wx2, wx2, [("te", B.cnt["te"]), ("sc", B.cnt["sc"]),
                             ("vt", B.cnt["vt"])])
    if stage >= 3:
        wx3 = [("vt", B.cnt["vt"])]
        ffn_call(wx3, wx3, [("te", B.cnt["te"]), ("sc", B.cnt["sc"]),
                            ("vt", B.cnt["vt"])])

    fin = [(s, B.cnt[s]) for s in ("dsc", "vt", "te", "sc", "ds", "dwi",
                                    "dw1a", "dw1b", "dw2a", "dw2b", "dza",
                                    "dzb", "dgz", "dgg", "gp")]
    xout_t = out_hbm.rearrange("(a two p) d -> two p a d", two=2, p=128)
    B.op("sync", lambda e: e.dma_start(xout_t[0], xv[0]), wait=fin, inc="do")
    B.op("sync", lambda e: e.dma_start(xout_t[1], xv[1]), inc="do")
    B.op("sync", None, wait=[("do", B.cnt["do"])])

    B.finalize()
    return nc


def _host_prep(L0_attn_W_qkv, L0_ffn_router_w, L0_ffn_w1, L0_ffn_b1,
               L0_ffn_w2, L0_ffn_b2):
    W = np.asarray(L0_attn_W_qkv, np.float32).reshape(8, 128, 3072).transpose(1, 0, 2)
    wq = np.ascontiguousarray(W[:, :, 0:1024].reshape(128, 8 * KB))
    wk = np.ascontiguousarray(W[:, :, 1024:2048].reshape(128, 8 * KB))
    wv = np.ascontiguousarray(W[:, :, 2048:3072].reshape(128, 8 * KB))
    rw = np.ascontiguousarray(
        np.asarray(L0_ffn_router_w, np.float32).reshape(8, 8, 128)
        .transpose(2, 1, 0).reshape(128, 64))
    w1 = np.ascontiguousarray(
        np.asarray(L0_ffn_w1, np.float32).reshape(E, 8, 128, 4, KB)
        .transpose(0, 3, 2, 1, 4).reshape(E, 4, 128, 8 * KB))
    w2 = np.ascontiguousarray(
        np.asarray(L0_ffn_w2, np.float32).reshape(E, 4, 8, 128, KB)
        .transpose(0, 1, 3, 2, 4).reshape(E, 4, 128, 8 * KB))
    b1 = np.ascontiguousarray(
        np.asarray(L0_ffn_b1, np.float32).reshape(E, 32, 128)
        .transpose(2, 0, 1).reshape(128, 256))
    b2 = np.ascontiguousarray(np.asarray(L0_ffn_b2, np.float32).reshape(1, E * KB))
    iota1 = ((np.arange(128)[:, None] % 16) * 128 + np.arange(128)[None, :]
             + 1).astype(np.float32)
    identf = np.eye(128, dtype=np.float32)
    identb = np.eye(128, dtype=np.float32).astype(ml_dtypes.bfloat16)
    repl8 = np.zeros((8, 128), np.float32)
    for e in range(8):
        repl8[e, 16 * e:16 * e + 16] = 1.0
    onesr = np.ones((1, 128), np.float32).astype(ml_dtypes.bfloat16)
    onesrf = np.ones((1, 128), np.float32)
    ones64 = np.ones((128, 1), np.float32).astype(ml_dtypes.bfloat16)
    extrm = np.zeros((128, 4), np.uint8)
    for g in range(8):
        extrm[16 * g + 1:16 * g + 16, 0] = 1
    return dict(wq_h=wq, wk_h=wk, wv_h=wv, rw_h=rw, w1_h=w1, w2_h=w2,
                b1_h=b1, b2_h=b2, iota1_h=iota1, identf_h=identf,
                identb_h=identb, repl8_h=repl8, onesr_h=onesr,
                onesrf_h=onesrf, ones64_h=ones64, extr_h=extrm)


def kernel(L0_attn_x, L0_attn_W_qkv, L0_ffn_router_w, L0_ffn_w1, L0_ffn_b1,
           L0_ffn_w2, L0_ffn_b2):
    x = np.ascontiguousarray(np.asarray(L0_attn_x, np.float32))
    common = _host_prep(L0_attn_W_qkv, L0_ffn_router_w, L0_ffn_w1, L0_ffn_b1,
                        L0_ffn_w2, L0_ffn_b2)
    nc = build_nc(stage=3)
    lower_extended_insts(nc)
    in_maps = [dict(common, x_in=x[b]) for b in range(8)]
    res = run_bass_kernel_spmd(nc, in_maps, core_ids=list(range(8)))
    return np.stack([np.asarray(r["y_out"]) for r in res.results], axis=0)
